# version 01 (final)
# baseline (speedup 1.0000x reference)
import numpy as np
import ml_dtypes
import concourse.bass as bass
import concourse.mybir as mybir
from concourse.bass_utils import run_bass_kernel_spmd

F32 = mybir.dt.float32
BF16 = mybir.dt.bfloat16
AF = mybir.ActivationFunctionType
ALU = mybir.AluOpType
AX = mybir.AxisListType
BF = ml_dtypes.bfloat16

NCORE = 8
T = 2048
D = 2048
NH = 16
HD = 128
NC = 16
C = 128
E = 32
I_FFN = 1408
NIB = 11
SI = 2816
NSB = 22
EPS_LN = 1e-5
EPS_RMS = 1e-6

_THETA_BITS = [1065353216, 1063104471, 1061157137, 1059470816, 1058010522, 1056527311, 1054337179, 1052440602, 1050798235, 1049376004, 1047712804, 1045579757, 1043732615, 1042133056, 1040747895, 1038909397, 1036831949, 1035032952, 1033475086, 1032126029, 1030116803, 1028093503, 1026341397, 1024824136, 1023510242, 1021334738, 1019364176, 1017657739, 1016180025, 1014779187, 1012562929, 1010643729, 1008981770, 1007542573, 1005959600, 1003801108, 1001931932, 1000313292, 998911607, 997151245, 995049015, 993228559, 991652109, 990286959, 988353832, 986306396, 984533390, 982998030, 981668463, 979567074, 977573004, 975846211, 974350870, 973033389, 970790694, 968848599, 967166815, 965710450, 964208660, 962024421, 960132947, 958494999, 957076594, 955395299]
_SLOPES_BITS = [1060439283, 1056964608, 1052050675, 1048576000, 1043662067, 1040187392, 1035273459, 1031798784, 1026884851, 1023410176, 1018496243, 1015021568, 1010107635, 1006632960, 1001719027, 998244352]
THETA = np.array(_THETA_BITS, np.uint32).view(np.float32)
LOG_LAM = (-np.array(_SLOPES_BITS, np.uint32).view(np.float32)).astype(np.float32)
LAM_C = np.exp(LOG_LAM * np.float32(C)).astype(np.float32)


class Plan:
    ENGINES = ("sync", "gpsimd", "scalar", "vector", "tensor")

    def __init__(self, nc):
        self.nc = nc
        self.entries = {e: [] for e in self.ENGINES}
        self.sems = {}
        self.cnt = {}

    def sem(self, name):
        self.sems[name] = self.nc.alloc_semaphore(name)
        self.cnt[name] = 0
        return name

    def wait(self, eng, sem, val=None):
        v = self.cnt[sem] if val is None else val
        if v > 0:
            self.entries[eng].append(("wait", sem, v, None))

    def add(self, eng, fn, inc=None, n=1):
        if inc is not None:
            self.cnt[inc] += n
        self.entries[eng].append(("op", inc, n, fn))

    def dma(self, eng, out, in_, sem, **kw):
        self.add(eng, lambda e, o=out, i=in_, k=kw: e.dma_start(out=o, in_=i, **k),
                 inc=sem, n=16)

    def check(self):
        ptr = {e: 0 for e in self.ENGINES}
        val = {s: 0 for s in self.sems}
        progress = True
        while progress:
            progress = False
            for e in self.ENGINES:
                while ptr[e] < len(self.entries[e]):
                    kind, sem, v, fn = self.entries[e][ptr[e]]
                    if kind == "wait" and val[sem] < v:
                        break
                    if kind == "op" and sem is not None:
                        val[sem] += v
                    ptr[e] += 1
                    progress = True
        stuck = {e: (ptr[e], len(self.entries[e])) for e in self.ENGINES
                 if ptr[e] < len(self.entries[e])}
        if stuck:
            lines = []
            for e, (p, tot) in stuck.items():
                kind, sem, v, _ = self.entries[e][p]
                lines.append(f"{e}@{p}/{tot}: waiting {sem}>={v} (final {val.get(sem)})")
            raise RuntimeError("PLAN DEADLOCK:\n" + "\n".join(lines))

    def emit(self, block):
        def body(name):
            def run(eng):
                for kind, sem, v, fn in self.entries[name]:
                    if kind == "wait":
                        eng.wait_ge(self.sems[sem], v)
                    else:
                        inst = fn(eng)
                        if sem is not None:
                            inst.then_inc(self.sems[sem], v)
            return run
        block.sync(body("sync"))
        block.gpsimd(body("gpsimd"))
        block.scalar(body("scalar"))
        block.vector(body("vector"))
        block.tensor(body("tensor"))


def build_program(stage=3, dbg=(), fsub=9):
    nc = bass.Bass("TRN2", target_bir_lowering=False)
    dt = nc.dram_tensor

    def EIN(name, shape, dtype=F32):
        return dt(name, shape, dtype, kind="ExternalInput").ap()

    def SCRATCH(name, shape, dtype):
        kind = "ExternalOutput" if name in dbg else "Internal"
        return dt(name, shape, dtype, kind=kind).ap()

    xT = EIN("xT", [D, T])
    wqH = EIN("wqH", [NH, 128, D])
    wkH = EIN("wkH", [NH, 128, D])
    wvH = EIN("wvH", [NH, 128, D])
    woH = EIN("woH", [NH, 128, D])
    wg1H = EIN("wg1H", [128, D])
    wg2 = EIN("wg2", [128, D])
    nwt = EIN("nwt", [1, D])
    cP = EIN("cP", [128, 16])
    wmodP = EIN("wmodP", [D, 6 * D])
    gwH = EIN("gwH", [128, 16 * E])
    rgH = EIN("rgH", [128, D])
    ruH = EIN("ruH", [128, D])
    rdH = EIN("rdH", [128, I_FFN])
    ugT = EIN("ugT", [E, 128, 128])
    uuT = EIN("uuT", [E, 128, 128])
    udT = EIN("udT", [E, 128, 128])
    cgT = EIN("cgT", [E, 128, I_FFN])
    cuT = EIN("cuT", [E, 128, I_FFN])
    cdT = EIN("cdT", [E, 128, D])
    srgH = EIN("srgH", [128, D])
    sruH = EIN("sruH", [128, D])
    srdH = EIN("srdH", [128, SI])
    sugT = EIN("sugT", [128, 128])
    suuT = EIN("suuT", [128, 128])
    sudT = EIN("sudT", [128, 128])
    scgT = EIN("scgT", [128, SI])
    scuT = EIN("scuT", [128, SI])
    scdT = EIN("scdT", [128, D])
    cosrep = EIN("cosrep", [128, T])
    sinrep = EIN("sinrep", [128, T])
    dtab = EIN("dtab", [128, NH * C])
    dqtab = EIN("dqtab", [128, NH * C])
    dktab = EIN("dktab", [128, NH])
    swts = EIN("swts", [128, 128])
    ebig = EIN("ebig", [E, E * 128])
    ident = EIN("ident", [128, 128])

    q_sc = SCRATCH("q_sc", [D, T], BF16)
    k_sc = SCRATCH("k_sc", [D, T], BF16)
    kt_sc = SCRATCH("kt_sc", [T, D], BF16)
    v_sc = SCRATCH("v_sc", [T, D], BF16)
    x2_sc = SCRATCH("x2_sc", [D, T], F32)
    mod_sc = dt("mod_sc", [6 * D], F32, kind="Internal").ap()
    sseg_sc = dt("sseg_sc", [128, NH * HD], F32, kind="Internal").ap()
    cc_out = dt("cc_out", [NCORE * 128, NH * HD], F32, kind="Internal", addr_space="Shared").ap()
    sin_sc = SCRATCH("sin_sc", [128, NH * HD], F32)
    h2_dbg = SCRATCH("h2_dbg", [D, T], BF16)
    rt_dbg = SCRATCH("rt_dbg", [128, 4608], F32)
    rd2_dbg = SCRATCH("rd2_dbg", [33 * 128, 1024], BF16)
    ln_dbg = SCRATCH("ln_dbg", [128, 3 * T], F32)
    o_dbg = SCRATCH("o_dbg", [T, D], BF16)
    g1_dbg = SCRATCH("g1_dbg", [128, T], BF16)
    sg_dbg = SCRATCH("sg_dbg", [T, D], BF16)
    sig_dbg = SCRATCH("sig_dbg", [T, D], BF16)
    on_dbg = SCRATCH("on_dbg", [T, D], F32)
    at_dbg = SCRATCH("at_dbg", [D, T], F32)
    gg_dbg = SCRATCH("gg_dbg", [128, 128], F32)
    yT = dt("yT", [D, T], F32, kind="ExternalOutput").ap()

    P = Plan(nc)
    sLd = P.sem("sLd")
    sGd = P.sem("sGd")
    sSd = P.sem("sSd")
    sPe = P.sem("sPe")
    sVe = P.sem("sVe")
    sSc = P.sem("sSc")
    sCc = P.sem("sCc")

    A = nc.alloc_sbuf_tensor

    mod_cols = A("mod_cols", [128, 96], F32)
    ca1 = A("ca1", [128, 16], F32)
    cb1 = A("cb1", [128, 16], F32)
    gac = A("gac", [128, 16], F32)
    gbc = A("gbc", [128, 16], F32)
    tmpg = A("tmpg", [128, 16], F32)
    S_sb = A("S_sb", [128, NH * HD], F32)
    S_bf = A("S_bf", [128, NH * HD], BF16)
    g1_all = A("g1_all", [128, T], BF16)
    nw_bc = A("nw_bc", [128, D], BF16)
    wg2_bf = A("wg2_bf", [128, D], BF16)
    dt_sb = A("dt_sb", [128, NH * C], F32)
    dq_sb = A("dq_sb", [128, NH * C], F32)
    dk_sb = A("dk_sb", [128, NH], F32)
    swts_sb = A("swts_sb", [128, 128], F32)
    id_sb = A("id_sb", [128, 128], F32)
    ones_col_bf = A("ones_col_bf", [128, 1], BF16)
    ones_row_f = A("ones_row_f", [1, 128], F32)
    siluc = A("siluc", [128, 16], BF16)
    siluc_f = A("siluc_f", [128, 16], F32)
    cP_sb = A("cP_sb", [128, 16], F32)
    kons = A("kons", [128, 4], F32)

    ps_a = nc.alloc_psum_tensor("ps_a", [128, 2048], F32)
    ps_b = nc.alloc_psum_tensor("ps_b", [128, 2048], F32)

    WF = [A(f"wf{i}", [128, T], F32) for i in range(4)]
    WB = [A(f"wb{i}", [128, T], BF16) for i in range(6)]
    r1 = A("r1", [1, T], F32)
    r2 = A("r2", [1, T], F32)
    r3 = A("r3", [1, T], F32)
    FB = A("fb", [128, 2048], BF16)
    mrow = r1
    h_bf = A("h_bf", [128, 16 * T], BF16)
    cs_cos = dt_sb
    cs_sin = dq_sb
    mean_bc = A("mean_bc", [128, T], BF16)
    rstd_bc = A("rstd_bc", [128, T], BF16)

    P.dma("sync", cP_sb[:, :], cP, sLd)
    P.dma("sync", dk_sb[:, :], dktab, sLd)
    P.dma("sync", swts_sb[:, :], swts, sLd)
    P.dma("sync", id_sb[:, :], ident, sLd)
    P.dma("sync", mrow[0:1, :], nwt, sLd)
    P.dma("sync", cs_cos[:, :], cosrep, sLd)
    P.dma("sync", cs_sin[:, :], sinrep, sLd)
    P.dma("gpsimd", wg2_bf[:, :], wg2, sGd)
    ld0 = P.cnt[sLd]
    P.wait("vector", sLd, ld0)
    P.add("vector", lambda e: e.memset(ones_col_bf[:, :], 1.0))
    P.add("vector", lambda e: e.memset(kons[:, 0:1], EPS_LN))
    P.add("vector", lambda e: e.memset(kons[:, 1:2], EPS_RMS))
    P.add("vector", lambda e: e.memset(kons[:, 2:3], -0.5))
    P.add("vector", lambda e: e.memset(kons[:, 3:4], 1.0))
    P.add("vector", lambda e: e.memset(ones_row_f[:, :], 1.0))
    P.add("vector", lambda e: e.memset(S_sb[:, :], 0.0), inc=sVe)
    P.wait("scalar", sLd, ld0)
    P.add("scalar", lambda e: e.activation(out=siluc_f[:, :], in_=cP_sb[:, :],
                                           func=AF.Silu))
    P.add("scalar", lambda e: e.activation(out=siluc[:, :], in_=siluc_f[:, :],
                                           func=AF.Copy), inc=sSc)

    P.wait("tensor", sVe, P.cnt[sVe])
    for g in range(4):
        P.add("tensor", lambda e, g=g: e.matmul(
            ps_a[:, g * 512:(g + 1) * 512], ones_row_f[0:1, :],
            mrow[0:1, g * 512:(g + 1) * 512], start=True, stop=True),
            inc=(sPe if g == 3 else None))
    P.wait("vector", sPe, P.cnt[sPe])
    P.add("vector", lambda e: e.tensor_copy(nw_bc[:, :], ps_a[:, :]), inc=sVe)
    nw_done_v = P.cnt[sVe]

    wm = WB[0:2]
    wmf = [WF[2], WF[3]]
    for sec in range(6):
        f32sec = sec in (2, 5)
        for kc in range(16):
            it = sec * 16 + kc
            i = it % 2
            if f32sec:
                if it >= 2:
                    P.wait("sync", sPe, P.cnt[sPe] - 1)
                P.dma("sync", wmf[i][:, :D],
                      wmodP[kc * 128:(kc + 1) * 128, sec * D:(sec + 1) * D], sLd)
                P.wait("tensor", sLd, P.cnt[sLd])
            else:
                if it >= 2:
                    P.wait("gpsimd", sPe, P.cnt[sPe] - 1)
                P.dma("gpsimd", wm[i][:, :D],
                      wmodP[kc * 128:(kc + 1) * 128, sec * D:(sec + 1) * D], sGd)
                P.wait("tensor", sGd, P.cnt[sGd])
            if it == 0:
                P.wait("tensor", sSc, 1)
            if kc == 0 and sec > 0:
                P.wait("tensor", sVe, P.cnt[sVe])
            for g in range(4):
                P.add("tensor", lambda e, i=i, kc=kc, g=g, f32sec=f32sec: e.matmul(
                    ps_b[0:1, g * 512:(g + 1) * 512],
                    (siluc_f if f32sec else siluc)[:, kc:kc + 1],
                    (wmf if f32sec else wm)[i][:, g * 512:(g + 1) * 512],
                    start=(kc == 0), stop=(kc == 15)),
                    inc=(sPe if g == 3 else None))
        P.wait("vector", sPe, P.cnt[sPe])
        P.add("vector", lambda e: e.tensor_copy(mrow[0:1, :], ps_b[0:1, :]), inc=sVe)
        P.wait("scalar", sVe, P.cnt[sVe])
        P.dma("scalar", mod_sc[sec * D:(sec + 1) * D], mrow[0:1, :], sSd)
    mod_rows_v = P.cnt[sVe]
    P.wait("sync", sSd, P.cnt[sSd])
    P.dma("sync", mod_cols[:, :].rearrange("p (s k) -> p s k", k=16),
          mod_sc.rearrange("(s p k) -> p s k", p=128, k=16), sLd)
    P.wait("vector", sLd, P.cnt[sLd])
    def vstep(fn):
        P.add("vector", fn, inc=sVe)
        P.wait("vector", sVe, P.cnt[sVe])
    vstep(lambda e: e.tensor_scalar(ca1[:, :], mod_cols[:, 16:32], 1.0, None, ALU.add))
    vstep(lambda e: e.tensor_scalar(cb1[:, :], mod_cols[:, 64:80], 1.0, None, ALU.add))
    for (a0, a1, dst) in ((32, 48, gac), (80, 96, gbc)):
        vstep(lambda e, a=a0, b=a1, d=dst: e.tensor_scalar(
            d[:, :], mod_cols[:, a:b], 0.0, None, ALU.is_ge))
        vstep(lambda e, d=dst: e.tensor_scalar(
            d[:, :], d[:, :], 2.0, -1.0, ALU.mult, ALU.add))
        vstep(lambda e, a=a0, b=a1: e.tensor_scalar(
            tmpg[:, :], mod_cols[:, a:b], -1.0, None, ALU.mult))
        vstep(lambda e, a=a0, b=a1: e.tensor_tensor(
            tmpg[:, :], tmpg[:, :], mod_cols[:, a:b], ALU.max))
        vstep(lambda e: e.tensor_scalar(
            tmpg[:, :], tmpg[:, :], 0.25, None, ALU.max))
        vstep(lambda e, d=dst: e.tensor_mul(d[:, :], d[:, :], tmpg[:, :]))
    mod_ready_v = P.cnt[sVe]
    if "gg_dbg" in dbg:
        P.wait("scalar", sVe, P.cnt[sVe])
        P.dma("scalar", gg_dbg[:, 0:16], gac[:, :], sSd)
        P.dma("scalar", gg_dbg[:, 16:32], gbc[:, :], sSd)
        P.dma("scalar", gg_dbg[:, 32:128], mod_cols[:, :], sSd)

    xs_b = WB[0:2]
    sq_b = WB[2:4]
    P.wait("tensor", sVe, mod_rows_v)
    P.wait("tensor", sVe, nw_done_v)
    for kc in range(16):
        i = kc % 2
        if kc >= 2:
            P.wait("gpsimd", sPe, P.cnt[sPe] - 1)
        P.dma("gpsimd", xs_b[i][:, :], xT[kc * 128:(kc + 1) * 128, :], sGd)
        P.wait("scalar", sGd, P.cnt[sGd])
        P.add("scalar", lambda e, i=i: e.activation(out=sq_b[i][:, :], in_=xs_b[i][:, :],
                                                    func=AF.Square), inc=sSc)
        P.wait("tensor", sGd, P.cnt[sGd])
        P.wait("tensor", sSc, P.cnt[sSc])
        for g in range(4):
            P.add("tensor", lambda e, i=i, g=g, kc=kc: e.matmul(
                ps_a[0:1, g * 512:(g + 1) * 512], ones_col_bf[:, 0:1],
                xs_b[i][:, g * 512:(g + 1) * 512], start=(kc == 0), stop=(kc == 15)))
        for g in range(4):
            P.add("tensor", lambda e, i=i, g=g, kc=kc: e.matmul(
                ps_b[0:1, g * 512:(g + 1) * 512], ones_col_bf[:, 0:1],
                sq_b[i][:, g * 512:(g + 1) * 512], start=(kc == 0), stop=(kc == 15)),
                inc=(sPe if g == 3 else None))
    P.wait("vector", sPe, P.cnt[sPe])
    P.add("vector", lambda e: e.tensor_scalar(r1[0:1, :], ps_a[0:1, :], 1.0 / D, None, ALU.mult))
    P.add("vector", lambda e: e.tensor_scalar(r2[0:1, :], ps_b[0:1, :], 1.0 / D, None, ALU.mult))
    P.add("vector", lambda e: e.tensor_tensor(r3[0:1, :], r1[0:1, :], r1[0:1, :], ALU.mult))
    P.add("vector", lambda e: e.tensor_sub(r2[0:1, :], r2[0:1, :], r3[0:1, :]), inc=sVe)
    P.wait("scalar", sVe, P.cnt[sVe])
    P.add("scalar", lambda e: e.activation(out=r3[0:1, :], in_=r2[0:1, :],
                                           func=AF.Ln, bias=kons[0:1, 0:1]))
    P.add("scalar", lambda e: e.activation(out=r3[0:1, :], in_=r3[0:1, :],
                                           func=AF.Exp, scale=kons[0:1, 2:3]), inc=sSc)
    P.wait("tensor", sVe, P.cnt[sVe])
    P.wait("tensor", sSc, P.cnt[sSc])
    for g in range(4):
        P.add("tensor", lambda e, g=g: e.matmul(
            ps_a[:, g * 512:(g + 1) * 512], ones_row_f[0:1, :],
            r1[0:1, g * 512:(g + 1) * 512], start=True, stop=True))
    for g in range(4):
        P.add("tensor", lambda e, g=g: e.matmul(
            ps_b[:, g * 512:(g + 1) * 512], ones_row_f[0:1, :],
            r3[0:1, g * 512:(g + 1) * 512], start=True, stop=True),
            inc=(sPe if g == 3 else None))
    P.wait("vector", sPe, P.cnt[sPe])
    P.add("vector", lambda e: e.tensor_copy(mean_bc[:, :], ps_a[:, :]))
    P.add("vector", lambda e: e.tensor_copy(rstd_bc[:, :], ps_b[:, :]), inc=sVe)
    bc_done_v = P.cnt[sVe]
    if "ln_dbg" in dbg:
        P.wait("scalar", sVe, bc_done_v)
        P.dma("gpsimd", ln_dbg[:, 0:T], mean_bc[:, :], sGd)
        P.dma("gpsimd", ln_dbg[:, T:2 * T], rstd_bc[:, :], sGd)
        P.dma("scalar", ln_dbg[0:1, 2 * T:3 * T], r3[0:1, :], sSd)
        P.wait("sync", sSd, P.cnt[sSd])

    xs_f = WF[0:2]
    t_f = WF[2]
    P.wait("vector", sVe, max(bc_done_v, mod_ready_v))
    for kc in range(16):
        i = kc % 2
        if kc >= 2:
            P.wait("sync", sVe, P.cnt[sVe] - 1)
        P.dma("sync", xs_f[i][:, :], xT[kc * 128:(kc + 1) * 128, :], sLd)
        P.wait("vector", sLd, P.cnt[sLd])
        P.add("vector", lambda e, i=i: e.tensor_tensor(t_f[:, :], xs_f[i][:, :],
                                                       mean_bc[:, :], ALU.subtract))
        P.add("vector", lambda e, kc=kc: e.scalar_tensor_tensor(
            t_f[:, :], t_f[:, :], ca1[:, kc:kc + 1], rstd_bc[:, :],
            ALU.mult, ALU.mult))
        P.add("vector", lambda e, kc=kc: e.tensor_scalar(
            h_bf[:, kc * T:(kc + 1) * T], t_f[:, :], mod_cols[:, kc:kc + 1],
            None, ALU.add), inc=sVe)
    h_done_v = P.cnt[sVe]

    wblk = WB[0:2]
    qk_si = WB[2]
    qk_sw = WB[3]
    rot_b = WB[4]
    evac_b = WB[5]
    m_t1 = WF[0]
    m_t2 = WF[1]
    fm_f = WF[2]
    for jh in range(NH):
        for which in range(3):
            w_ap = (wqH, wkH, wvH)[which]
            it = jh * 3 + which
            i = it % 2
            pg = (ps_a, ps_b)[it % 2]
            if it >= 2:
                P.wait("gpsimd", sPe, P.cnt[sPe] - 1)
            P.dma("gpsimd", wblk[i][:, :D], w_ap[jh], sGd)
            P.wait("vector", sLd, P.cnt[sLd])
            P.wait("tensor", sGd, P.cnt[sGd])
            if it == 0:
                P.wait("tensor", sVe, h_done_v)
            P.wait("tensor", sVe, P.cnt[sVe])
            P.wait("tensor", sSc, P.cnt[sSc])
            for kc in range(16):
                for g in range(4):
                    P.add("tensor", lambda e, i=i, kc=kc, g=g, pg=pg: e.matmul(
                        pg[:, g * 512:(g + 1) * 512],
                        wblk[i][:, kc * 128:(kc + 1) * 128],
                        h_bf[:, kc * T + g * 512:kc * T + (g + 1) * 512],
                        start=(kc == 0), stop=(kc == 15)),
                        inc=(sPe if (kc == 15 and g == 3) else None))
            if which in (0, 1):
                P.wait("scalar", sPe, P.cnt[sPe])
                P.wait("scalar", sVe, P.cnt[sVe])
                P.add("scalar", lambda e, pg=pg: e.activation(
                    out=qk_si[:, :], in_=pg[:, :], func=AF.Silu), inc=sSc)
                P.wait("vector", sSc, P.cnt[sSc])
                P.add("vector", lambda e: e.tensor_copy(qk_sw[0:64, :], qk_si[64:128, :]))
                P.add("vector", lambda e: e.tensor_copy(qk_sw[64:128, :], qk_si[0:64, :]))
                P.add("vector", lambda e: e.tensor_mul(m_t1[:, :], qk_si[:, :], cs_cos[:, :]))
                P.add("vector", lambda e: e.tensor_mul(m_t2[:, :], qk_sw[:, :], cs_sin[:, :]))
                P.add("vector", lambda e: e.tensor_sub(rot_b[0:64, :], m_t1[0:64, :],
                                                       m_t2[0:64, :]))
                P.add("vector", lambda e: e.tensor_add(rot_b[64:128, :], m_t2[64:128, :],
                                                       m_t1[64:128, :]), inc=sVe)
                P.wait("sync", sVe, P.cnt[sVe])
                P.dma("sync", (q_sc, k_sc)[which][jh * 128:(jh + 1) * 128, :],
                      rot_b[:, :], sLd)
                if which == 1:
                    P.add("vector", lambda e: e.tensor_sub(fm_f[0:64, :], m_t1[0:64, :],
                                                           m_t2[0:64, :]))
                    P.add("vector", lambda e: e.tensor_add(fm_f[64:128, :], m_t2[64:128, :],
                                                           m_t1[64:128, :]), inc=sVe)
            else:
                P.wait("vector", sPe, P.cnt[sPe])
                P.add("vector", lambda e, pg=pg: e.tensor_copy(fm_f[:, :], pg[:, :]),
                      inc=sVe)
            if which in (1, 2):
                po = (ps_b, ps_a)[it % 2]
                P.wait("tensor", sVe, P.cnt[sVe])
                for cc2 in range(16):
                    P.add("tensor", lambda e, cc2=cc2, po=po: e.transpose(
                        po[:, cc2 * 128:(cc2 + 1) * 128],
                        fm_f[:, cc2 * 128:(cc2 + 1) * 128], id_sb[:, :]),
                        inc=(sPe if cc2 == 15 else None))
                P.wait("vector", sPe, P.cnt[sPe])
                if which == 1:
                    P.add("vector", lambda e, jh=jh, po=po: e.tensor_scalar(
                        evac_b[:, :], po[:, :], dk_sb[:, jh:jh + 1], None, ALU.mult),
                        inc=sVe)
                else:
                    P.add("vector", lambda e, po=po: e.tensor_copy(evac_b[:, :], po[:, :]),
                          inc=sVe)
                P.wait("sync", sVe, P.cnt[sVe])
                P.dma("sync",
                      (kt_sc, v_sc)[which - 1][:, jh * 128:(jh + 1) * 128]
                      .rearrange("(c p) d -> p c d", p=128),
                      evac_b[:, :].rearrange("p (c d) -> p c d", d=128), sLd)
    heads_ld = P.cnt[sLd]

    P.wait("gpsimd", sPe, P.cnt[sPe])
    P.dma("gpsimd", wblk[0][:, :D], wg1H, sGd)
    P.wait("tensor", sGd, P.cnt[sGd])
    P.wait("tensor", sVe, P.cnt[sVe])
    for kc in range(16):
        for g in range(4):
            P.add("tensor", lambda e, kc=kc, g=g: e.matmul(
                ps_a[:, g * 512:(g + 1) * 512],
                wblk[0][:, kc * 128:(kc + 1) * 128],
                h_bf[:, kc * T + g * 512:kc * T + (g + 1) * 512],
                start=(kc == 0), stop=(kc == 15)),
                inc=(sPe if (kc == 15 and g == 3) else None))
    P.wait("vector", sPe, P.cnt[sPe])
    P.add("vector", lambda e: e.tensor_copy(g1_all[:, :], ps_a[:, :]), inc=sVe)
    if "g1_dbg" in dbg:
        P.wait("scalar", sVe, P.cnt[sVe])
        P.dma("scalar", g1_dbg, g1_all[:, :], sSd)

    vc_b = WB[0:2]
    ktc_b = WB[2:4]
    P.wait("sync", sPe, P.cnt[sPe])
    for cc in range(16):
        i = cc % 2
        if cc >= 2:
            P.wait("sync", sVe, P.cnt[sVe] - 1)
        P.wait("sync", sLd, heads_ld)
        P.dma("sync", vc_b[i][:, :NH * HD], v_sc[cc * 128:(cc + 1) * 128, :], sLd)
        P.dma("sync", ktc_b[i][:, :NH * HD], kt_sc[cc * 128:(cc + 1) * 128, :], sLd)
        P.wait("tensor", sLd, P.cnt[sLd])
        P.wait("tensor", sVe, P.cnt[sVe])
        for j in range(NH):
            P.add("tensor", lambda e, i=i, j=j: e.matmul(
                ps_b[:, j * 128:(j + 1) * 128],
                ktc_b[i][:, j * 128:(j + 1) * 128],
                vc_b[i][:, j * 128:(j + 1) * 128],
                start=True, stop=True),
                inc=(sPe if j == NH - 1 else None))
        P.wait("vector", sPe, P.cnt[sPe])
        for j in range(NH):
            P.add("vector", lambda e, j=j: e.scalar_tensor_tensor(
                S_sb[:, j * 128:(j + 1) * 128], S_sb[:, j * 128:(j + 1) * 128],
                float(LAM_C[j]), ps_b[:, j * 128:(j + 1) * 128],
                ALU.mult, ALU.add), inc=(sVe if j == NH - 1 else None))

    P.wait("scalar", sVe, P.cnt[sVe])
    P.dma("scalar", sseg_sc, S_sb[:, :], sSd)
    P.wait("gpsimd", sSd, P.cnt[sSd])
    P.add("gpsimd", lambda e: e.collective_compute(
        "AllGather", ALU.bypass, replica_groups=[list(range(NCORE))],
        ins=[sseg_sc.opt()], outs=[cc_out.opt()]), inc=sCc, n=1)
    P.wait("vector", sCc, P.cnt[sCc])
    P.add("vector", lambda e: e.memset(S_sb[:, :], 0.0), inc=sVe)
    sm_f = WF[0:2]
    for m in range(NCORE):
        i = m % 2
        if m >= 2:
            P.wait("sync", sVe, P.cnt[sVe] - 1)
        P.wait("sync", sCc, P.cnt[sCc])
        P.dma("sync", sm_f[i][:, :NH * HD], cc_out[m * 128:(m + 1) * 128, :], sLd)
        P.wait("vector", sLd, P.cnt[sLd])
        for j in range(NH):
            P.add("vector", lambda e, i=i, m=m, j=j: e.scalar_tensor_tensor(
                S_sb[:, j * 128:(j + 1) * 128], sm_f[i][:, j * 128:(j + 1) * 128],
                swts_sb[:, m * NH + j:m * NH + j + 1],
                S_sb[:, j * 128:(j + 1) * 128], ALU.mult, ALU.add),
                inc=(sVe if j == NH - 1 else None))
    P.add("vector", lambda e: e.tensor_copy(S_bf[:, :], S_sb[:, :]), inc=sVe)
    sin_done_v = P.cnt[sVe]
    if "sin_sc" in dbg:
        P.wait("scalar", sVe, sin_done_v)
        P.dma("scalar", sin_sc, S_sb[:, :], sSd)

    if stage >= 2:
        P.wait("sync", sVe, P.cnt[sVe])
        P.dma("sync", dt_sb[:, :], dtab, sLd)
        P.dma("sync", dq_sb[:, :], dqtab, sLd)
        tabs_ld = P.cnt[sLd]
        HS = [h_bf[:, i * T:(i + 1) * T] for i in range(16)]
        qc_b = [HS[0], HS[1]]
        kc_b = [HS[2], HS[3]]
        ktc_b2 = [HS[4], HS[5]]
        vc_b2 = [HS[6], HS[7]]
        sig_bf = HS[8]
        ats_bf = HS[9]
        qdec_bf = HS[10]
        onT_g = h_bf[:, 11 * T:15 * T]
        sq_bf = HS[15]
        osig_g = WF[0:4]
        ms_all = A("ms_all", [128, 64], F32)
        rstd_all = A("rstd_all", [128, 64], F32)
        woq = [A(f"woq{i}", [128, 512], BF16) for i in range(2)]
        xf_t = [A(f"xf_t{i}", [128, 512], F32) for i in range(2)]
        x2b = A("x2b", [128, 512], F32)

        for gi in range(4):
            for cg in range(4):
                cc = gi * 4 + cg
                i = cc % 2
                if cc == 0:
                    P.wait("sync", sPe, P.cnt[sPe])
                if cc >= 2:
                    P.wait("sync", sPe, P.cnt[sPe] - 2)
                P.dma("sync", qc_b[i], q_sc[:, cc * 128:(cc + 1) * 128]
                      .rearrange("(j p) t -> p j t", p=128), sLd)
                P.dma("sync", kc_b[i], k_sc[:, cc * 128:(cc + 1) * 128]
                      .rearrange("(j p) t -> p j t", p=128), sLd)
                P.dma("sync", ktc_b2[i], kt_sc[cc * 128:(cc + 1) * 128, :], sLd)
                P.dma("sync", vc_b2[i], v_sc[cc * 128:(cc + 1) * 128, :], sLd)
                ld_c = P.cnt[sLd]
                P.wait("tensor", sVe, P.cnt[sVe])
                P.wait("tensor", sSc, P.cnt[sSc])
                for g in range(4):
                    P.add("tensor", lambda e, cc=cc, g=g: e.matmul(
                        ps_a[:, g * 512:(g + 1) * 512],
                        g1_all[:, cc * 128:(cc + 1) * 128],
                        wg2_bf[:, g * 512:(g + 1) * 512],
                        start=True, stop=True),
                        inc=(sPe if g == 3 else None))
                P.wait("scalar", sPe, P.cnt[sPe])
                P.add("scalar", lambda e: e.activation(
                    out=sig_bf, in_=ps_a[:, :], func=AF.Sigmoid), inc=sSc)
                if "sg_dbg" in dbg:
                    P.wait("scalar", sSc, P.cnt[sSc])
                    P.dma("scalar", sg_dbg[cc * 128:(cc + 1) * 128, :], sig_bf, sSd)
                    P.wait("scalar", sSd, P.cnt[sSd])
                P.wait("tensor", sLd, ld_c)
                P.wait("tensor", sSc, P.cnt[sSc])
                if cc == 0:
                    P.wait("tensor", sVe, sin_done_v)
                    P.wait("tensor", sLd, tabs_ld)
                for j in range(NH):
                    P.add("tensor", lambda e, i=i, j=j: e.matmul(
                        ps_a[:, j * 128:(j + 1) * 128],
                        kc_b[i][:, j * 128:(j + 1) * 128],
                        qc_b[i][:, j * 128:(j + 1) * 128],
                        start=True, stop=True),
                        inc=(sPe if j == NH - 1 else None))
                P.wait("vector", sPe, P.cnt[sPe])
                P.add("vector", lambda e: e.tensor_tensor(
                    ats_bf, ps_a[:, :], dt_sb[:, :], ALU.mult))
                P.add("vector", lambda e, i=i: e.tensor_tensor(
                    qdec_bf, qc_b[i], dq_sb[:, :], ALU.mult), inc=sVe)
                P.wait("tensor", sVe, P.cnt[sVe])
                for j in range(NH):
                    P.add("tensor", lambda e, i=i, j=j: e.matmul(
                        ps_b[:, j * 128:(j + 1) * 128],
                        ats_bf[:, j * 128:(j + 1) * 128],
                        vc_b2[i][:, j * 128:(j + 1) * 128],
                        start=True, stop=False))
                    P.add("tensor", lambda e, i=i, j=j: e.matmul(
                        ps_b[:, j * 128:(j + 1) * 128],
                        qdec_bf[:, j * 128:(j + 1) * 128],
                        S_bf[:, j * 128:(j + 1) * 128],
                        start=False, stop=True),
                        inc=(sPe if j == NH - 1 else None))
                o_pe = P.cnt[sPe]
                for j in range(NH):
                    P.add("tensor", lambda e, i=i, j=j: e.matmul(
                        ps_a[:, j * 128:(j + 1) * 128],
                        ktc_b2[i][:, j * 128:(j + 1) * 128],
                        vc_b2[i][:, j * 128:(j + 1) * 128],
                        start=True, stop=True),
                        inc=(sPe if j == NH - 1 else None))
                kv_pe = P.cnt[sPe]
                P.wait("vector", sPe, o_pe)
                if "o_dbg" in dbg:
                    P.add("vector", lambda e: e.tensor_copy(ats_bf, ps_b[:, :]), inc=sVe)
                    P.wait("scalar", sVe, P.cnt[sVe])
                    P.dma("scalar", o_dbg[cc * 128:(cc + 1) * 128, :], ats_bf, sSd)
                    P.wait("vector", sSd, P.cnt[sSd])
                    P.wait("scalar", sSd, P.cnt[sSd])
                P.add("vector", lambda e, cg=cg: e.tensor_tensor(
                    osig_g[cg][:, :], ps_b[:, :], sig_bf, ALU.mult), inc=sVe)
                P.wait("vector", sPe, kv_pe)
                for j in range(NH):
                    P.add("vector", lambda e, j=j: e.scalar_tensor_tensor(
                        S_sb[:, j * 128:(j + 1) * 128], S_sb[:, j * 128:(j + 1) * 128],
                        float(LAM_C[j]), ps_a[:, j * 128:(j + 1) * 128],
                        ALU.mult, ALU.add))
                P.add("vector", lambda e: e.tensor_copy(S_bf[:, :], S_sb[:, :]), inc=sVe)
                P.wait("scalar", sVe, P.cnt[sVe])
                P.add("scalar", lambda e, cg=cg: e.activation(
                    out=sq_bf, in_=osig_g[cg][:, :], func=AF.Square), inc=sSc)
                P.wait("vector", sSc, P.cnt[sSc])
                P.add("vector", lambda e, cg=cg: e.reduce_sum(
                    ms_all[:, cg * 16:(cg + 1) * 16].rearrange("p (a b) -> p a b", b=1),
                    sq_bf.rearrange("p (a b) -> p a b", b=128), axis=AX.X), inc=sVe)

            P.wait("vector", sVe, P.cnt[sVe])
            P.add("vector", lambda e: e.tensor_scalar(
                rstd_all[:, :], ms_all[:, :], 1.0 / HD, EPS_RMS, ALU.mult, ALU.add),
                inc=sVe)
            P.wait("vector", sVe, P.cnt[sVe])
            P.add("vector", lambda e: e.reciprocal(rstd_all[:, :], rstd_all[:, :]),
                  inc=sVe)
            P.wait("scalar", sVe, P.cnt[sVe])
            P.add("scalar", lambda e: e.activation(
                out=rstd_all[:, :], in_=rstd_all[:, :], func=AF.Sqrt), inc=sSc)
            P.wait("vector", sSc, P.cnt[sSc])
            for cg in range(4):
                for j in range(NH):
                    P.add("vector", lambda e, cg=cg, j=j: e.scalar_tensor_tensor(
                        osig_g[cg][:, j * 128:(j + 1) * 128],
                        osig_g[cg][:, j * 128:(j + 1) * 128],
                        rstd_all[:, cg * 16 + j:cg * 16 + j + 1],
                        nw_bc[:, j * 128:(j + 1) * 128], ALU.mult, ALU.mult),
                        inc=(sVe if j == NH - 1 else None))
                if "on_dbg" in dbg:
                    P.wait("scalar", sVe, P.cnt[sVe])
                    P.dma("scalar", on_dbg[(gi * 4 + cg) * 128:(gi * 4 + cg + 1) * 128, :],
                          osig_g[cg][:, :], sSd)
                    P.wait("tensor", sSd, P.cnt[sSd])
                P.wait("tensor", sVe, P.cnt[sVe])
                for j in range(NH):
                    P.add("tensor", lambda e, cg=cg, j=j: e.transpose(
                        ps_a[:, j * 128:(j + 1) * 128],
                        osig_g[cg][:, j * 128:(j + 1) * 128], id_sb[:, :]),
                        inc=(sPe if j == NH - 1 else None))
                P.wait("vector", sPe, P.cnt[sPe])
                P.add("vector", lambda e, cg=cg: e.tensor_copy(
                    onT_g.rearrange("p (j t) -> p j t", t=512)[:, :, cg * 128:(cg + 1) * 128],
                    ps_a[:, :].rearrange("p (j t) -> p j t", t=128)), inc=sVe)
            for qt in range(4):
                P.wait("tensor", sVe, P.cnt[sVe])
                for j in range(NH):
                    it = gi * 4 * NH + qt * NH + j
                    i = it % 2
                    if it >= 2:
                        P.wait("gpsimd", sPe, P.cnt[sPe] - 1)
                    P.dma("gpsimd", woq[i][:, :], woH[j][:, qt * 512:(qt + 1) * 512], sGd)
                    P.wait("tensor", sGd, P.cnt[sGd])
                    for f2l in range(4):
                        P.add("tensor", lambda e, i=i, j=j, f2l=f2l: e.matmul(
                            ps_b[:, f2l * 512:(f2l + 1) * 512],
                            woq[i][:, f2l * 128:(f2l + 1) * 128],
                            onT_g[:, j * 512:(j + 1) * 512],
                            start=(j == 0), stop=(j == NH - 1)),
                            inc=(sPe if f2l == 3 else None))
                for f2l in range(4):
                    f2 = qt * 4 + f2l
                    ii = f2 % 2
                    if gi * 16 + f2 >= 2:
                        P.wait("sync", sVe, P.cnt[sVe] - 1)
                    P.dma("sync", xf_t[ii][:, :],
                          xT[f2 * 128:(f2 + 1) * 128, gi * 512:(gi + 1) * 512], sLd)
                    P.wait("vector", sLd, P.cnt[sLd])
                    P.wait("vector", sPe, P.cnt[sPe])
                    P.wait("vector", sSd, P.cnt[sSd])
                    if "at_dbg" in dbg:
                        P.wait("vector", sSd, P.cnt[sSd])
                        P.add("vector", lambda e, f2l=f2l: e.tensor_copy(
                            x2b[:, :], ps_b[:, f2l * 512:(f2l + 1) * 512]), inc=sVe)
                        P.wait("scalar", sVe, P.cnt[sVe])
                        P.dma("scalar", at_dbg[f2 * 128:(f2 + 1) * 128,
                                               gi * 512:(gi + 1) * 512], x2b[:, :], sSd)
                        P.wait("vector", sSd, P.cnt[sSd])
                    P.add("vector", lambda e, f2=f2, f2l=f2l, ii=ii: e.scalar_tensor_tensor(
                        x2b[:, :], ps_b[:, f2l * 512:(f2l + 1) * 512],
                        gac[:, f2:f2 + 1], xf_t[ii][:, :], ALU.mult, ALU.add),
                        inc=sVe)
                    P.wait("scalar", sVe, P.cnt[sVe])
                    P.dma("scalar", x2_sc[f2 * 128:(f2 + 1) * 128,
                                          gi * 512:(gi + 1) * 512], x2b[:, :], sSd)
        x2_done_sd = P.cnt[sSd]

    if stage >= 3:
        HS2 = [h_bf[:, i * T:(i + 1) * T] for i in range(16)]
        xs_b2 = WB[0:2]
        sq_b2 = WB[2:4]
        xs_f2 = WF[0:2]
        t_f2 = WF[2]
        P.wait("gpsimd", sSd, x2_done_sd)
        P.wait("tensor", sVe, P.cnt[sVe])
        P.wait("gpsimd", sPe, P.cnt[sPe])
        for kc in range(16):
            i = kc % 2
            if kc >= 2:
                P.wait("gpsimd", sPe, P.cnt[sPe] - 1)
            P.dma("gpsimd", xs_b2[i][:, :], x2_sc[kc * 128:(kc + 1) * 128, :], sGd)
            P.wait("scalar", sGd, P.cnt[sGd])
            P.add("scalar", lambda e, i=i: e.activation(out=sq_b2[i][:, :], in_=xs_b2[i][:, :],
                                                        func=AF.Square), inc=sSc)
            P.wait("tensor", sGd, P.cnt[sGd])
            P.wait("tensor", sSc, P.cnt[sSc])
            for g in range(4):
                P.add("tensor", lambda e, i=i, g=g, kc=kc: e.matmul(
                    ps_a[0:1, g * 512:(g + 1) * 512], ones_col_bf[:, 0:1],
                    xs_b2[i][:, g * 512:(g + 1) * 512], start=(kc == 0), stop=(kc == 15)))
            for g in range(4):
                P.add("tensor", lambda e, i=i, g=g, kc=kc: e.matmul(
                    ps_b[0:1, g * 512:(g + 1) * 512], ones_col_bf[:, 0:1],
                    sq_b2[i][:, g * 512:(g + 1) * 512], start=(kc == 0), stop=(kc == 15)),
                    inc=(sPe if g == 3 else None))
        P.wait("vector", sPe, P.cnt[sPe])
        P.add("vector", lambda e: e.tensor_scalar(r1[0:1, :], ps_a[0:1, :], 1.0 / D, None, ALU.mult))
        P.add("vector", lambda e: e.tensor_scalar(r2[0:1, :], ps_b[0:1, :], 1.0 / D, None, ALU.mult))
        P.add("vector", lambda e: e.tensor_tensor(r3[0:1, :], r1[0:1, :], r1[0:1, :], ALU.mult))
        P.add("vector", lambda e: e.tensor_sub(r2[0:1, :], r2[0:1, :], r3[0:1, :]), inc=sVe)
        P.wait("scalar", sVe, P.cnt[sVe])
        P.add("scalar", lambda e: e.activation(out=r3[0:1, :], in_=r2[0:1, :],
                                               func=AF.Ln, bias=kons[0:1, 0:1]))
        P.add("scalar", lambda e: e.activation(out=r3[0:1, :], in_=r3[0:1, :],
                                               func=AF.Exp, scale=kons[0:1, 2:3]), inc=sSc)
        P.wait("tensor", sVe, P.cnt[sVe])
        P.wait("tensor", sSc, P.cnt[sSc])
        for g in range(4):
            P.add("tensor", lambda e, g=g: e.matmul(
                ps_a[:, g * 512:(g + 1) * 512], ones_row_f[0:1, :],
                r1[0:1, g * 512:(g + 1) * 512], start=True, stop=True))
        for g in range(4):
            P.add("tensor", lambda e, g=g: e.matmul(
                ps_b[:, g * 512:(g + 1) * 512], ones_row_f[0:1, :],
                r3[0:1, g * 512:(g + 1) * 512], start=True, stop=True),
                inc=(sPe if g == 3 else None))
        P.wait("vector", sPe, P.cnt[sPe])
        P.add("vector", lambda e: e.tensor_copy(mean_bc[:, :], ps_a[:, :]))
        P.add("vector", lambda e: e.tensor_copy(rstd_bc[:, :], ps_b[:, :]), inc=sVe)
        bc2_v = P.cnt[sVe]
        P.wait("vector", sVe, bc2_v)
        P.wait("sync", sPe, P.cnt[sPe])
        for kc in range(16):
            i = kc % 2
            if kc >= 2:
                P.wait("sync", sVe, P.cnt[sVe] - 1)
            P.dma("sync", xs_f2[i][:, :], x2_sc[kc * 128:(kc + 1) * 128, :], sLd)
            P.wait("vector", sLd, P.cnt[sLd])
            P.add("vector", lambda e, i=i: e.tensor_tensor(t_f2[:, :], xs_f2[i][:, :],
                                                           mean_bc[:, :], ALU.subtract))
            P.add("vector", lambda e, kc=kc: e.scalar_tensor_tensor(
                t_f2[:, :], t_f2[:, :], cb1[:, kc:kc + 1], rstd_bc[:, :],
                ALU.mult, ALU.mult))
            P.add("vector", lambda e, kc=kc: e.tensor_scalar(
                HS2[kc], t_f2[:, :], mod_cols[:, 48 + kc:48 + kc + 1],
                None, ALU.add), inc=sVe)
        h2_v = P.cnt[sVe]
        if "h2_dbg" in dbg:
            P.wait("scalar", sVe, h2_v)
            for kc in range(16):
                P.dma("scalar", h2_dbg[kc * 128:(kc + 1) * 128, :], HS2[kc], sSd)
        if fsub < 2:
            P.wait("sync", sSd, P.cnt[sSd])
            P.wait("sync", sGd, P.cnt[sGd])
            P.wait("sync", sLd, P.cnt[sLd])
            P.check()
            with nc.Block() as block:
                P.emit(block)
            return nc

        rg_bf = WB[0]
        ru_bf = WB[1]
        rgs_bf = WB[2]
        rus_bf = WB[3]
        rkw = FB[:, 0:2048]
        for (w_ap, dst) in ((rgH, rg_bf), (ruH, ru_bf), (srgH, rgs_bf), (sruH, rus_bf)):
            P.wait("gpsimd", sPe, P.cnt[sPe])
            P.dma("gpsimd", rkw, w_ap, sGd)
            P.wait("tensor", sGd, P.cnt[sGd])
            P.wait("tensor", sVe, P.cnt[sVe])
            for kc in range(16):
                for g in range(4):
                    P.add("tensor", lambda e, kc=kc, g=g: e.matmul(
                        ps_a[:, g * 512:(g + 1) * 512], rkw[:, kc * 128:(kc + 1) * 128],
                        HS2[kc][:, g * 512:(g + 1) * 512], start=(kc == 0), stop=(kc == 15)),
                        inc=(sPe if (kc == 15 and g == 3) else None))
            P.wait("vector", sPe, P.cnt[sPe])
            P.add("vector", lambda e, dst=dst: e.tensor_copy(dst[:, :], ps_a[:, :]), inc=sVe)
        ranks_v = P.cnt[sVe]
        if "rk_dbg" in dbg:
            P.wait("gpsimd", sVe, ranks_v)
            P.dma("gpsimd", rt_dbg[:, 2560:4608], rg_bf[:, :], sGd)
        if fsub < 3:
            P.wait("sync", sSd, P.cnt[sSd])
            P.wait("sync", sGd, P.cnt[sGd])
            P.wait("sync", sLd, P.cnt[sLd])
            P.check()
            with nc.Block() as block:
                P.emit(block)
            return nc

        gw_sb = woq[0][:, 0:512]
        lg_sb = WF[2][:, 0:512]
        e_sb = WF[2][:, 512:1024]
        ew_sb = WF[2][:, 1024:1536]
        we_tm = WF[2][:, 1536:2048]
        mx16 = ms_all
        we_fm = WB[4][0:32, :]
        P.dma("gpsimd", gw_sb, gwH, sGd)
        P.wait("tensor", sGd, P.cnt[sGd])
        for cq in range(4):
            P.wait("tensor", sVe, P.cnt[sVe])
            for c4 in range(4):
                c = cq * 4 + c4
                for kc in range(16):
                    P.add("tensor", lambda e, c=c, c4=c4, kc=kc: e.matmul(
                        ps_a[:, c4 * 512:c4 * 512 + 32],
                        HS2[kc][:, c * 128:(c + 1) * 128],
                        gw_sb[:, kc * E:(kc + 1) * E],
                        start=(kc == 0), stop=(kc == 15)),
                        inc=(sPe if (kc == 15 and c4 == 3) else None))
            P.wait("vector", sPe, P.cnt[sPe])
            for c4 in range(4):
                P.add("vector", lambda e, cq=cq, c4=c4: e.tensor_copy(
                    lg_sb[:, cq * 128 + c4 * E:cq * 128 + (c4 + 1) * E],
                    ps_a[:, c4 * 512:c4 * 512 + E]),
                    inc=(sVe if c4 == 3 else None))

        def vsync():
            P.wait("vector", sVe, P.cnt[sVe])

        def vstep2(fn):
            P.add("vector", fn, inc=sVe)
            vsync()

        seg = lambda ap: ap.rearrange("p (a b) -> p a b", b=E)
        vstep2(lambda e: e.reduce_max(mx16[:, 0:16].rearrange("p (a b) -> p a b", b=1),
                                      seg(lg_sb), axis=AX.X, negate=True))
        P.wait("scalar", sVe, P.cnt[sVe])
        for c in range(16):
            P.add("scalar", lambda e, c=c: e.activation(
                out=e_sb[:, c * 32:(c + 1) * 32], in_=lg_sb[:, c * 32:(c + 1) * 32],
                func=AF.Exp, bias=mx16[:, c:c + 1]), inc=(sSc if c == 15 else None))
        P.wait("vector", sSc, P.cnt[sSc])
        vstep2(lambda e: e.tensor_copy(ew_sb, e_sb))
        for it3 in range(3):
            vstep2(lambda e: e.reduce_max(mx16[:, 16:32].rearrange("p (a b) -> p a b", b=1),
                                          seg(ew_sb), axis=AX.X))
            vstep2(lambda e: e.tensor_tensor(
                we_tm, seg_in := seg(ew_sb), None, ALU.bypass) if False else
                e.tensor_tensor(seg(we_tm), seg(ew_sb),
                                mx16[:, 16:32].rearrange("p (a b) -> p a b", b=1)
                                .broadcast_to([128, 16, E]), ALU.is_ge))
            vstep2(lambda e: e.tensor_scalar(we_tm, we_tm, -1.0, 1.0, ALU.mult, ALU.add))
            vstep2(lambda e: e.tensor_mul(ew_sb, ew_sb, we_tm))
        vstep2(lambda e: e.reduce_max(mx16[:, 16:32].rearrange("p (a b) -> p a b", b=1),
                                      seg(ew_sb), axis=AX.X))
        vstep2(lambda e: e.tensor_tensor(seg(we_tm), seg(e_sb),
                                         mx16[:, 16:32].rearrange("p (a b) -> p a b", b=1)
                                         .broadcast_to([128, 16, E]), ALU.is_ge))
        vstep2(lambda e: e.tensor_mul(we_tm, we_tm, e_sb))
        vstep2(lambda e: e.reduce_sum(mx16[:, 32:48].rearrange("p (a b) -> p a b", b=1),
                                      seg(we_tm), axis=AX.X))
        vstep2(lambda e: e.reciprocal(mx16[:, 32:48], mx16[:, 32:48]))
        vstep2(lambda e: e.tensor_tensor(seg(we_tm), seg(we_tm),
                                         mx16[:, 32:48].rearrange("p (a b) -> p a b", b=1)
                                         .broadcast_to([128, 16, E]), ALU.mult))
        P.wait("tensor", sVe, P.cnt[sVe])
        for c in range(16):
            P.add("tensor", lambda e, c=c: e.transpose(
                ps_b[0:32, c * 128:(c + 1) * 128], we_tm[:, c * 32:(c + 1) * 32],
                id_sb[:, :]), inc=(sPe if c == 15 else None))
        P.wait("vector", sPe, P.cnt[sPe])
        P.add("vector", lambda e: e.tensor_copy(we_fm[:, :], ps_b[0:32, :]), inc=sVe)
        route_v = P.cnt[sVe]
        if "rt_dbg" in dbg:
            P.wait("scalar", sVe, route_v)
            P.dma("scalar", rt_dbg[:, 0:512], we_tm, sSd)
            P.dma("gpsimd", rt_dbg[0:32, 512:2560], we_fm, sGd)
            P.dma("scalar", rt_dbg[:, 2560:3072], lg_sb, sSd)
            P.dma("scalar", rt_dbg[:, 3072:3584], e_sb, sSd)
            P.dma("scalar", rt_dbg[:, 3584:3648], mx16[:, 0:64], sSd)
        if fsub < 4:
            P.wait("sync", sSd, P.cnt[sSd])
            P.wait("sync", sGd, P.cnt[sGd])
            P.wait("sync", sLd, P.cnt[sLd])
            P.check()
            with nc.Block() as block:
                P.emit(block)
            return nc

        wug = [wg2_bf[:, 0:128], wg2_bf[:, 128:256]]
        wuu = [wg2_bf[:, 256:384], wg2_bf[:, 384:512]]
        wud = [wg2_bf[:, 512:640], wg2_bf[:, 640:768]]
        ebg = [wg2_bf[:, 768:896], wg2_bf[:, 896:1024]]
        wcg = [mean_bc[:, 0:I_FFN], rstd_bc[:, 0:I_FFN]]
        wcu = [g1_all[:, 0:I_FFN], nw_bc[:, 0:I_FFN]]
        wsg = [g1_all[:, 0:SI] if False else None, None]
        pg_bf = S_bf[:, 0:512]
        pu_bf = S_bf[:, 512:1024]
        web = S_bf[:, 1024:1536]
        rd_bf = S_bf[:, 1536:2048]
        gib = [WB[5][:, 0:512], WB[5][:, 512:1024]]
        gub = [WB[5][:, 1024:1536], WB[5][:, 1536:2048]]
        rdh_sb = FB[:, 0:I_FFN]
        rd2_s = wg2_bf[:, 1024:2048]
        cdq = [woq[0][:, 0:256], woq[1][:, 0:256]]
        yx = [WF[3][:, 0:512], WF[3][:, 512:1024]]

        P.wait("gpsimd", sVe, route_v)

        for half in range(2):
            hoff = half * 1024
            P.wait("gpsimd", sPe, P.cnt[sPe])
            P.dma("gpsimd", rdh_sb[:, :], rdH, sGd)
            for ee in range(33):
                shared = (ee == 32)
                i = ee % 2
                nib = NSB if shared else NIB
                if half + ee > 0:
                    P.wait("gpsimd", sPe, P.cnt[sPe])
                if shared:
                    P.dma("gpsimd", wug[i], sugT, sGd)
                    P.dma("gpsimd", wuu[i], suuT, sGd)
                    P.dma("gpsimd", wud[i], sudT, sGd)
                    P.dma("gpsimd", wcg[i], scgT[:, 0:I_FFN], sGd)
                    P.dma("gpsimd", wcu[i], scuT[:, 0:I_FFN], sGd)
                    P.dma("gpsimd", rdh_sb[:, :], srdH[:, 0:I_FFN], sGd)
                else:
                    P.dma("gpsimd", wug[i], ugT[ee], sGd)
                    P.dma("gpsimd", wuu[i], uuT[ee], sGd)
                    P.dma("gpsimd", wud[i], udT[ee], sGd)
                    P.dma("gpsimd", wcg[i], cgT[ee], sGd)
                    P.dma("gpsimd", wcu[i], cuT[ee], sGd)
                    P.dma("gpsimd", ebg[i][0:32, :], ebig[:, ee * 128:(ee + 1) * 128], sGd)
                w_ld = P.cnt[sGd]
                for tt in range(2):
                    toff = hoff + tt * 512
                    if shared and tt == 1:
                        P.wait("gpsimd", sPe, P.cnt[sPe])
                        P.dma("gpsimd", wcg[i], scgT[:, 0:I_FFN], sGd)
                        P.dma("gpsimd", wcu[i], scuT[:, 0:I_FFN], sGd)
                        P.dma("gpsimd", rdh_sb[:, :], srdH[:, 0:I_FFN], sGd)
                        w_ld = P.cnt[sGd]
                    P.wait("tensor", sGd, w_ld)
                    P.wait("tensor", sVe, P.cnt[sVe])
                    P.wait("tensor", sSc, P.cnt[sSc])
                    src_g = (rgs_bf if shared else rg_bf)
                    src_u = (rus_bf if shared else ru_bf)
                    P.add("tensor", lambda e, i=i, toff=toff, src_g=src_g: e.matmul(
                        ps_b[:, 0:512], wug[i], src_g[:, toff:toff + 512],
                        start=True, stop=True))
                    P.add("tensor", lambda e, i=i, toff=toff, src_u=src_u: e.matmul(
                        ps_b[:, 1024:1536], wuu[i], src_u[:, toff:toff + 512],
                        start=True, stop=True), inc=(sPe if shared else None))
                    if not shared:
                        P.add("tensor", lambda e, i=i, toff=toff: e.matmul(
                            ps_b[:, 512:1024], ebg[i][0:32, :],
                            we_fm[:, toff:toff + 512], start=True, stop=True), inc=sPe)
                    P.wait("vector", sPe, P.cnt[sPe])
                    P.add("vector", lambda e: e.tensor_copy(pg_bf, ps_b[:, 0:512]))
                    if shared:
                        P.add("vector", lambda e: e.tensor_copy(pu_bf, ps_b[:, 1024:1536]),
                              inc=sVe)
                    else:
                        P.add("vector", lambda e: e.tensor_copy(web, ps_b[:, 512:1024]),
                              inc=sVe)
                        vsync()
                        P.add("vector", lambda e: e.tensor_tensor(
                            pu_bf, ps_b[:, 1024:1536], web, ALU.mult), inc=sVe)
                    for ib in range(nib):
                        pia = (ib % 2) * 512
                        pib = 1024 + (ib % 2) * 512
                        ibw = ib % NIB
                        if shared and ib == NIB:
                            P.wait("gpsimd", sPe, P.cnt[sPe])
                            P.dma("gpsimd", wcg[i], scgT[:, I_FFN:SI], sGd)
                            P.dma("gpsimd", wcu[i], scuT[:, I_FFN:SI], sGd)
                            P.dma("gpsimd", rdh_sb[:, :], srdH[:, I_FFN:SI], sGd)
                            P.wait("tensor", sGd, P.cnt[sGd])
                        if ib >= 2:
                            P.wait("tensor", sVe, P.cnt[sVe] - 1)
                        if ib == 0:
                            P.wait("tensor", sVe, P.cnt[sVe])
                        P.add("tensor", lambda e, i=i, ibw=ibw, pia=pia: e.matmul(
                            ps_a[:, pia:pia + 512],
                            wcg[i][:, ibw * 128:(ibw + 1) * 128],
                            pg_bf, start=True, stop=True), inc=sPe)
                        P.add("tensor", lambda e, i=i, ibw=ibw, pib=pib: e.matmul(
                            ps_a[:, pib:pib + 512],
                            wcu[i][:, ibw * 128:(ibw + 1) * 128],
                            pu_bf, start=True, stop=True), inc=sPe)
                        P.wait("scalar", sPe, P.cnt[sPe] - 1)
                        P.wait("scalar", sVe, P.cnt[sVe])
                        P.add("scalar", lambda e, ib=ib, pia=pia: e.activation(
                            out=gib[ib % 2], in_=ps_a[:, pia:pia + 512], func=AF.Silu),
                            inc=sSc)
                        P.wait("vector", sSc, P.cnt[sSc])
                        P.wait("vector", sPe, P.cnt[sPe])
                        P.add("vector", lambda e, ib=ib, pib=pib: e.tensor_tensor(
                            gub[ib % 2], ps_a[:, pib:pib + 512], gib[ib % 2], ALU.mult),
                            inc=sVe)
                        P.wait("tensor", sVe, P.cnt[sVe])
                        P.add("tensor", lambda e, ibw=ibw, ib=ib, nib=nib: e.matmul(
                            ps_b[:, 1536:2048], rdh_sb[:, ibw * 128:(ibw + 1) * 128],
                            gub[ib % 2], start=(ib == 0), stop=(ib == nib - 1)),
                            inc=(sPe if ib == nib - 1 else None))
                    P.wait("vector", sPe, P.cnt[sPe])
                    P.add("vector", lambda e: e.tensor_copy(rd_bf, ps_b[:, 1536:2048]),
                          inc=sVe)
                    P.wait("tensor", sVe, P.cnt[sVe])
                    P.add("tensor", lambda e, i=i: e.matmul(
                        ps_b[:, 0:512], wud[i], rd_bf, start=True, stop=True), inc=sPe)
                    P.wait("vector", sPe, P.cnt[sPe])
                    dst = (rd2_s[:, tt * 512:(tt + 1) * 512] if shared
                           else h_bf[:, (ee * 1024) + tt * 512:(ee * 1024) + (tt + 1) * 512])
                    P.add("vector", lambda e, dst=dst: e.tensor_copy(dst, ps_b[:, 0:512]),
                          inc=sVe)

            if "rd2_dbg" in dbg and half == 0:
                P.wait("scalar", sVe, P.cnt[sVe])
                for ee2 in range(33):
                    P.dma("scalar", rd2_dbg[ee2 * 128:(ee2 + 1) * 128, :],
                          (rd2_s if ee2 == 32 else h_bf[:, ee2 * 1024:(ee2 + 1) * 1024]), sSd)
            if fsub < 5:
                P.wait("sync", sSd, P.cnt[sSd])
                P.wait("sync", sGd, P.cnt[sGd])
                P.wait("sync", sLd, P.cnt[sLd])
                P.check()
                with nc.Block() as block:
                    P.emit(block)
                return nc
            rd2_of = lambda ee2, tt2: (rd2_s[:, tt2 * 512:(tt2 + 1) * 512] if ee2 == 32
                                       else h_bf[:, ee2 * 1024 + tt2 * 512:ee2 * 1024 + (tt2 + 1) * 512])
            for p8 in range(8):
                P.wait("tensor", sVe, P.cnt[sVe])
                for ee in range(33):
                    i = (p8 * 33 + ee) % 2
                    if p8 * 33 + ee >= 2 or half > 0:
                        P.wait("gpsimd", sPe, P.cnt[sPe] - 1)
                    src_cd = (scdT[:, p8 * 256:(p8 + 1) * 256] if ee == 32
                              else cdT[ee][:, p8 * 256:(p8 + 1) * 256])
                    P.dma("gpsimd", cdq[i][:, :], src_cd, sGd)
                    P.wait("tensor", sGd, P.cnt[sGd])
                    for l in range(2):
                        for tt in range(2):
                            P.add("tensor", lambda e, i=i, l=l, tt=tt, ee=ee: e.matmul(
                                ps_a[:, (l * 2 + tt) * 512:(l * 2 + tt + 1) * 512],
                                cdq[i][:, l * 128:(l + 1) * 128],
                                rd2_of(ee, tt), start=(ee == 0), stop=(ee == 32)),
                                inc=(sPe if (l == 1 and tt == 1) else None))
                for l in range(2):
                    f2 = p8 * 2 + l
                    for tt in range(2):
                        ii = (f2 * 2 + tt) % 2
                        toff = hoff + tt * 512
                        if half * 32 + f2 * 2 + tt >= 2:
                            P.wait("sync", sVe, P.cnt[sVe] - 1)
                        P.dma("sync", yx[ii][:, :],
                              x2_sc[f2 * 128:(f2 + 1) * 128, toff:toff + 512], sLd)
                        P.wait("vector", sLd, P.cnt[sLd])
                        P.wait("vector", sPe, P.cnt[sPe])
                        P.wait("vector", sSd, P.cnt[sSd])
                        P.add("vector", lambda e, l=l, tt=tt, f2=f2, ii=ii: e.scalar_tensor_tensor(
                            x2b[:, :], ps_a[:, (l * 2 + tt) * 512:(l * 2 + tt + 1) * 512],
                            gbc[:, f2:f2 + 1], yx[ii][:, :], ALU.mult, ALU.add), inc=sVe)
                        P.wait("scalar", sVe, P.cnt[sVe])
                        P.dma("scalar", yT[f2 * 128:(f2 + 1) * 128, toff:toff + 512],
                              x2b[:, :], sSd)
        P.wait("sync", sSd, P.cnt[sSd])

    P.wait("sync", sSd, P.cnt[sSd])
    P.wait("sync", sGd, P.cnt[sGd])
    P.wait("sync", sLd, P.cnt[sLd])
    P.check()
    with nc.Block() as block:
        P.emit(block)
    return nc


def _tables(cid):
    pos = np.arange(16384, dtype=np.float32)[cid * T:(cid + 1) * T]
    ang = pos[:, None] * THETA[None, :]
    cos = np.cos(ang).astype(np.float32).T
    sin = np.sin(ang).astype(np.float32).T
    cosrep = np.ascontiguousarray(np.concatenate([cos, cos], axis=0))
    sinrep = np.ascontiguousarray(np.concatenate([sin, sin], axis=0))
    i = np.arange(C, dtype=np.float32)
    diff = i[:, None] - i[None, :]
    dtab = np.zeros((128, NH * C), np.float32)
    dqtab = np.zeros((128, NH * C), np.float32)
    for j in range(NH):
        Dj = np.where(diff >= 0, np.exp((LOG_LAM[j] * diff).astype(np.float32)), 0.0)
        dtab[:, j * C:(j + 1) * C] = Dj.T.astype(np.float32)
        dq = np.exp((LOG_LAM[j] * (i + np.float32(1.0))).astype(np.float32))
        dqtab[:, j * C:(j + 1) * C] = np.broadcast_to(dq[None, :], (128, C))
    dk = np.exp((LOG_LAM[None, :] * (np.float32(C) - np.float32(1.0) - i[:, None]))
                .astype(np.float32)).astype(np.float32)
    dktab = np.ascontiguousarray(dk)
    swts = np.zeros((128, 128), np.float32)
    for m in range(NCORE):
        for j in range(NH):
            if m < cid:
                swts[:, m * NH + j] = np.float32(
                    float(LAM_C[j]) ** (NC * (cid - 1 - m)))
    ebig = np.zeros((E, E * 128), np.float32)
    for e in range(E):
        ebig[e, e * 128:(e + 1) * 128] = 1.0
    return dict(cosrep=cosrep, sinrep=sinrep, dtab=dtab, dqtab=dqtab,
                dktab=dktab, swts=swts, ebig=ebig,
                ident=np.eye(128, dtype=np.float32))


def _blocks16(w):
    K, N = w.shape
    cN = K // 128
    return np.ascontiguousarray(
        w.reshape(cN, 128, N).transpose(1, 0, 2).reshape(128, cN * N))


def _head_blocks(w):
    return np.ascontiguousarray(
        w.reshape(16, 128, 16, 128).transpose(2, 1, 0, 3).reshape(16, 128, 2048))


def prepare_inputs(ins):
    f32 = np.float32
    base = {}
    base["wqH"] = _head_blocks(np.asarray(ins["L0_attn_Wq"], f32))
    base["wkH"] = _head_blocks(np.asarray(ins["L0_attn_Wk"], f32))
    base["wvH"] = _head_blocks(np.asarray(ins["L0_attn_Wv"], f32))
    base["woH"] = np.ascontiguousarray(np.asarray(ins["L0_attn_Wo"], f32)
                                       .reshape(16, 128, 2048))
    base["wg1H"] = _blocks16(np.asarray(ins["L0_attn_Wg1"], f32))
    base["wg2"] = np.ascontiguousarray(np.asarray(ins["L0_attn_Wg2"], f32))
    base["nwt"] = np.asarray(ins["L0_attn_norm_weight"], f32).reshape(1, D)
    base["cP"] = np.ascontiguousarray(np.asarray(ins["L0_c"], f32).reshape(16, 128).T)
    base["wmodP"] = np.ascontiguousarray(
        np.asarray(ins["L0_wmod"], f32).reshape(D, 6, 16, 128)
        .transpose(0, 1, 3, 2).reshape(D, 6 * D))
    base["gwH"] = _blocks16(np.ascontiguousarray(np.asarray(ins["L0_ffn_gate_w"], f32).T))
    base["rgH"] = _blocks16(np.ascontiguousarray(np.asarray(ins["L0_ffn_Rg"], f32).T))
    base["ruH"] = _blocks16(np.ascontiguousarray(np.asarray(ins["L0_ffn_Ru"], f32).T))
    base["rdH"] = _blocks16(np.ascontiguousarray(np.asarray(ins["L0_ffn_Rd"], f32).T))
    for nm, key in (("ugT", "L0_ffn_Ug"), ("uuT", "L0_ffn_Uu"), ("udT", "L0_ffn_Ud"),
                    ("cgT", "L0_ffn_Cg"), ("cuT", "L0_ffn_Cu"), ("cdT", "L0_ffn_Cd")):
        base[nm] = np.ascontiguousarray(np.asarray(ins[key], f32).transpose(0, 2, 1))
    base["srgH"] = _blocks16(np.ascontiguousarray(np.asarray(ins["L0_ffn_s_Rg"], f32).T))
    base["sruH"] = _blocks16(np.ascontiguousarray(np.asarray(ins["L0_ffn_s_Ru"], f32).T))
    base["srdH"] = _blocks16(np.ascontiguousarray(np.asarray(ins["L0_ffn_s_Rd"], f32).T))
    base["sugT"] = np.ascontiguousarray(np.asarray(ins["L0_ffn_s_Ug"], f32).T)
    base["suuT"] = np.ascontiguousarray(np.asarray(ins["L0_ffn_s_Uu"], f32).T)
    base["sudT"] = np.ascontiguousarray(np.asarray(ins["L0_ffn_s_Ud"], f32).T)
    base["scgT"] = np.ascontiguousarray(np.asarray(ins["L0_ffn_s_Cg"], f32).T)
    base["scuT"] = np.ascontiguousarray(np.asarray(ins["L0_ffn_s_Cu"], f32).T)
    base["scdT"] = np.ascontiguousarray(np.asarray(ins["L0_ffn_s_Cd"], f32).T)
    xT_full = np.ascontiguousarray(np.asarray(ins["L0_attn_x"], f32)[0].T)
    in_maps = []
    for cid in range(NCORE):
        m = dict(base)
        m.update(_tables(cid))
        m["xT"] = np.ascontiguousarray(xT_full[:, cid * T:(cid + 1) * T])
        in_maps.append(m)
    return in_maps


def kernel(**ins):
    in_maps = prepare_inputs(ins)
    nc = build_program(stage=3)
    res = run_bass_kernel_spmd(nc, in_maps, core_ids=list(range(NCORE)))
    outs = [r["yT"] for r in res.results]
    full = np.concatenate(outs, axis=1)
    return np.ascontiguousarray(full.T)[None]
